# version 19
# baseline (speedup 1.0000x reference)
import numpy as np
import concourse.bass as bass
import concourse.mybir as mybir
from concourse.bass_utils import run_bass_kernel_spmd

F32 = mybir.dt.float32
BF16 = mybir.dt.bfloat16
I32 = mybir.dt.int32
AF = mybir.ActivationFunctionType
ALU = mybir.AluOpType

DM = 4096
T = 4096
NH = 32
HD = 128
HFF = 11008
HT = HFF // 128
DT = DM // 128
KT = T // 128
TC = T // 512
NQC = 2
RK = 16
SCALE = 1.0 / float(np.sqrt(HD))
INV_D = 1.0 / DM
EPS = 1e-5

AR_COLS = 64512
QX_C = 0
MZ_C = 16384
ARW_C = 32768
TR_C = 49152
WK_C = TR_C
WV_C = TR_C + 4096
XKV_C = TR_C + 8192
WS_C = [TR_C, TR_C + 4096]
QT_C = [TR_C + 8192, TR_C + 10240]
MB_C = TR_C + 8704
RB_C = TR_C + 9216
TM_C = TR_C + 9728
PM_C = [TR_C + 10752, TR_C + 12800]
ZQ_C = TR_C + 14848
HID_C = 0
GS_C = [44032, 48128]
US_C = [52224, 56320]
SIL_C = [60416, 60928]
DS_C = [44032 + 1024 * i for i in range(4)]
KTS_C = 0
VS_C = 4096


def build_program(n_heads=NH, n_qc=NQC, n_tc=TC, n_ht=HT):
    nc = bass.Bass("TRN2", target_bir_lowering=False)

    xT = nc.dram_tensor("xT", [DM, T], F32, kind="ExternalInput").ap()
    xq = nc.dram_tensor("xq", [128, 2 * DT * 512], F32,
                        kind="ExternalInput").ap()
    wq = nc.dram_tensor("wq", [128, NH * DM], F32, kind="ExternalInput").ap()
    wk = nc.dram_tensor("wk", [128, DT * HD], F32, kind="ExternalInput").ap()
    wv = nc.dram_tensor("wv", [128, DT * HD], F32, kind="ExternalInput").ap()
    wo = nc.dram_tensor("wo", [128, DT * DM], F32, kind="ExternalInput").ap()
    gwT = nc.dram_tensor("gwT", [128, HT * DM], F32, kind="ExternalInput").ap()
    uwT = nc.dram_tensor("uwT", [128, HT * DM], F32, kind="ExternalInput").ap()
    dwT = nc.dram_tensor("dwT", [HFF, DM], F32, kind="ExternalInput").ap()
    gaT = nc.dram_tensor("gaT", [128, DT * RK], F32, kind="ExternalInput").ap()
    gbT = nc.dram_tensor("gbT", [RK, HFF], F32, kind="ExternalInput").ap()
    uaT = nc.dram_tensor("uaT", [128, DT * RK], F32, kind="ExternalInput").ap()
    ubT = nc.dram_tensor("ubT", [RK, HFF], F32, kind="ExternalInput").ap()
    daT = nc.dram_tensor("daT", [128, HT * RK], F32, kind="ExternalInput").ap()
    dbT = nc.dram_tensor("dbT", [RK, DM], F32, kind="ExternalInput").ap()
    thr = nc.dram_tensor("thr", [128, 64], F32, kind="ExternalInput").ap()
    out = nc.dram_tensor("out", [DM, 1024], F32, kind="ExternalOutput").ap()

    arena = nc.alloc_sbuf_tensor("arena", [128, AR_COLS], BF16)
    slaba = nc.alloc_sbuf_tensor("slaba", [128, 8192], BF16)
    y1sb = nc.alloc_sbuf_tensor("y1sb", [128, 16384], BF16)
    D_i32 = nc.alloc_sbuf_tensor("D_i32", [128, 512], I32)
    D_f32 = nc.alloc_sbuf_tensor("D_f32", [128, 512], F32)
    thr_sb = nc.alloc_sbuf_tensor("thr_sb", [128, 64], F32)
    ones_c = nc.alloc_sbuf_tensor("ones_c", [128, 1], BF16)
    ones_r = nc.alloc_sbuf_tensor("ones_r", [1, 128], F32)
    gaT_sb = nc.alloc_sbuf_tensor("gaT_sb", [128, DT * RK], BF16)
    uaT_sb = nc.alloc_sbuf_tensor("uaT_sb", [128, DT * RK], BF16)
    daT_sb = nc.alloc_sbuf_tensor("daT_sb", [128, HT * RK], BF16)
    gb_sb = nc.alloc_sbuf_tensor("gb_sb", [16, 2 * 128], BF16)
    ub_sb = nc.alloc_sbuf_tensor("ub_sb", [16, 2 * 128], BF16)
    db_sb = nc.alloc_sbuf_tensor("db_sb", [16, 1024], BF16)
    xag_sb = nc.alloc_sbuf_tensor("xag_sb", [16, 512], BF16)
    xau_sb = nc.alloc_sbuf_tensor("xau_sb", [16, 512], BF16)
    xad_sb = nc.alloc_sbuf_tensor("xad_sb", [16, 512], BF16)
    rowA = nc.alloc_sbuf_tensor("rowA", [1, 512], F32)
    rowAb = nc.alloc_sbuf_tensor("rowAb", [1, 512], BF16)
    ones_rb = nc.alloc_sbuf_tensor("ones_rb", [1, 128], BF16)
    rowB = nc.alloc_sbuf_tensor("rowB", [1, 512], F32)
    rowC = nc.alloc_sbuf_tensor("rowC", [1, 512], F32)
    tmpf = nc.alloc_sbuf_tensor("tmpf", [128, 512], F32)
    outb = [nc.alloc_sbuf_tensor(f"outb{i}", [128, 512], F32) for i in range(2)]

    PS4 = nc.alloc_psum_tensor("ps0123", [128, 2048], F32)
    P4 = nc.alloc_psum_tensor("ps4", [128, 512], F32)
    P5 = nc.alloc_psum_tensor("ps5", [128, 512], F32)
    P6 = nc.alloc_psum_tensor("ps6", [128, 512], F32)
    P7 = nc.alloc_psum_tensor("ps7", [128, 512], F32)
    Px = [P4, P5, P6, P7]

    def BK(i, r=128):
        if i < 4:
            return PS4[0:r, i * 512:(i + 1) * 512]
        return Px[i - 4][0:r, :]

    sems = {}
    for name in ["dq", "sm", "sv", "ss", "sg",
                 "xk0", "xk1", "xk2", "xk3", "xk4", "xk5", "xk6", "xk7",
                 "ws0", "ws1",
                 "gs0", "gs1", "us0", "us1", "gb0", "gb1", "ub0", "ub1",
                 "ds0", "ds1", "ds2", "ds3", "db0", "so0", "so1"]:
        sems[name] = nc.alloc_semaphore(name)
    dq, sm, sv, ss, sg = (sems[k] for k in ["dq", "sm", "sv", "ss", "sg"])

    cnt = {k: 0 for k in sems}
    waited = {}
    prog = {k: [] for k in "tvsgy"}

    def emit(eng, fn):
        prog[eng].append(fn)

    def wait(eng, ch, val):
        if not val or val <= 0:
            return
        key = (eng, ch)
        if waited.get(key, 0) >= val:
            return
        waited[key] = val
        s = sems[ch]
        emit(eng, lambda e, s=s, v=val: e.wait_ge(s, v))

    def bump(ch, k=1):
        cnt[ch] += k
        return cnt[ch]

    def g_dma(dst, src, ch="dq"):
        n_ = bump(ch, 16)
        sem = sems[ch]
        emit("g", lambda g, d=dst, s=src, se=sem:
             g.dma_start(d, s).then_inc(se, 16))
        return n_

    def y_dma(dst, src, ch):
        n_ = bump(ch, 16)
        sem = sems[ch]
        emit("y", lambda y, d=dst, s=src, se=sem:
             y.dma_start(d, s).then_inc(se, 16))
        return n_

    def t_mm(o, l, r, st, sp, inc=True, tp=None):
        if inc:
            n_ = bump("sm")
            emit("t", lambda t, o=o, l=l, r=r, a=st, b=sp, c=tp:
                 t.matmul(o, l, r, start=a, stop=b,
                          tile_position=c).then_inc(sm, 1))
            return n_
        emit("t", lambda t, o=o, l=l, r=r, a=st, b=sp, c=tp:
             t.matmul(o, l, r, start=a, stop=b, tile_position=c))
        return None

    def v_op(name, *aps, inc=True):
        if inc:
            n_ = bump("sv")
            emit("v", lambda v, nm=name, a=aps: getattr(v, nm)(*a).then_inc(sv, 1))
            return n_
        emit("v", lambda v, nm=name, a=aps: getattr(v, nm)(*a))
        return None

    def s_act(o, i, func, scale=1.0, inc=True):
        if inc:
            n_ = bump("ss")
            emit("s", lambda s, o=o, i=i, f=func, sc=scale:
                 s.activation(o, i, f, scale=sc).then_inc(ss, 1))
            return n_
        emit("s", lambda s, o=o, i=i, f=func, sc=scale:
             s.activation(o, i, f, scale=sc))
        return None

    def s_copy(o, i):
        n_ = bump("ss")
        emit("s", lambda s, o=o, i=i: s.copy(o, i).then_inc(ss, 1))
        return n_

    def ar(c0, w=512):
        return arena[:, c0:c0 + w]

    def ln_pass(zt, out_write, mid_cb=None):
        st_mm = 0
        zq_free = [0, 0]
        zq_c = [ZQ_C, SIL_C[0]]
        sq = PS4[64:65, 1536:2048]
        for d2 in range(DT):
            jz = d2 % 2
            wait("v", "sm", zq_free[jz])
            zq = v_op("tensor_mul", ar(zq_c[jz]), zt(d2), zt(d2))
            t_mm(BK(2, 1), ones_c[:], zt(d2), d2 == 0, d2 == DT - 1,
                 inc=False, tp=(0, 0))
            wait("t", "sv", zq)
            st_mm = t_mm(sq, ones_c[:], ar(zq_c[jz]),
                         d2 == 0, d2 == DT - 1, tp=(0, 64))
            zq_free[jz] = st_mm
        if mid_cb is not None:
            mid_cb()
        wait("v", "sm", st_mm)
        va = v_op("tensor_scalar", rowA[:], BK(2, 1), INV_D, None, ALU.mult)
        vb = v_op("tensor_scalar", rowB[:], sq, INV_D, None, ALU.mult)
        wait("v", "sv", va)
        vc = v_op("tensor_mul", rowC[:], rowA[:], rowA[:])
        wait("v", "sv", max(vb, vc))
        vd = v_op("tensor_sub", rowB[:], rowB[:], rowC[:])
        wait("v", "sv", vd)
        rv = v_op("tensor_scalar", rowB[:], rowB[:], EPS, None, ALU.add)
        wait("s", "sv", rv)
        ln1 = s_act(rowC[:], rowB[:], AF.Ln)
        wait("s", "ss", ln1)
        rr = s_act(rowB[:], rowC[:], AF.Exp, scale=-0.5)
        wait("t", "sv", rv)
        wait("t", "ss", rr)
        t_mm(P4[:], ones_r[:], rowA[:], True, True, inc=False)
        bb = t_mm(P5[:], ones_r[:], rowB[:], True, True)
        wait("v", "sm", bb)
        p4f = v_op("tensor_copy", ar(MB_C), P4[:])
        p5f = v_op("tensor_copy", ar(RB_C), P5[:])
        last = 0
        for d2 in range(DT):
            last = out_write(d2)
        return last, p4f, p5f

    xkv_free = [0, 0, 0]
    pair_evac = [0, 0]
    ws_free = [0, 0]
    qp_free = 0
    s_free = 0
    pm_free = [0, 0]
    pv_free = 0
    den_free = 0
    den_free_sv = 0
    bc_free = 0
    o_free = [0, 0]
    gu_slab_free = [0, 0]
    gu_psum_free = [0, 0]
    up_psum_free = [0, 0]
    sil_free = [0, 0]
    bt_free = [0, 0]
    ds_free = [0, 0, 0, 0]
    db_free = 0
    bank_free = [0] * 8
    ob_free = [0, 0]
    qx_free = 0
    mz_free = 0

    bump("sg")
    emit("g", lambda g: g.iota(D_i32[:], [[4, 512]], base=0,
                               channel_multiplier=-1).then_inc(sg, 1))
    bump("sg")
    emit("g", lambda g: g.memset(ones_c[:], 1.0).then_inc(sg, 1))
    bump("sg")
    emit("g", lambda g: g.memset(ones_r[:], 1.0).then_inc(sg, 1))
    bump("sg")
    emit("g", lambda g: g.memset(ones_rb[:], 1.0).then_inc(sg, 1))
    sg_setup = cnt["sg"]

    g_dma(thr_sb[:], thr[:, :])
    g_dma(arena[:, WK_C:WK_C + 4096], wk[:, :])
    g_dma(arena[:, WV_C:WV_C + 4096], wv[:, :])
    g_dma(gaT_sb[:], gaT[:, :])
    g_dma(uaT_sb[:], uaT[:, :])
    g_dma(daT_sb[:], daT[:, :])
    d_thr = cnt["dq"]

    wait("v", "sg", sg_setup)
    d_copy = v_op("tensor_copy", D_f32[:], D_i32[:])

    kv_ready = 0
    for tc in range(n_tc):
        pair = tc % 2
        pK, pV = BK(pair * 2), BK(pair * 2 + 1)
        wait("t", "dq", cnt["dq"])
        wait("t", "sg", sg_setup)
        first = True
        for dg in range(DT // 4):
            slot = (tc * (DT // 4) + dg) % 3
            wait("g", "sm", xkv_free[slot])
            src = xT[dg * 512:(dg + 1) * 512,
                     tc * 512:(tc + 1) * 512].rearrange(
                         "(a p) c -> p a c", p=128)
            dd = g_dma(ar(XKV_C + slot * 2048,
                          2048).rearrange("p (a c) -> p a c", c=512),
                       src, f"xk{slot}")
            wait("t", f"xk{slot}", dd)
            if first:
                wait("t", "sv", pair_evac[pair])
                first = False
            for di in range(4):
                d = dg * 4 + di
                xkv = ar(XKV_C + slot * 2048 + di * 512)
                t_mm(pK, arena[:, WK_C + d * 128:WK_C + (d + 1) * 128], xkv,
                     d == 0, d == DT - 1, inc=False)
                for tt in range(4):
                    mmn = t_mm(pV[:, tt * 128:(tt + 1) * 128],
                               xkv[:, tt * 128:(tt + 1) * 128],
                               arena[:, WV_C + d * 128:WV_C + (d + 1) * 128],
                               (d == 0 and tt == 0),
                               (d == DT - 1 and tt == 3), inc=(tt == 3))
            xkv_free[slot] = mmn
        wait("v", "sm", mmn)
        v_op("tensor_copy", slaba[:, KTS_C + tc * 512:KTS_C + (tc + 1) * 512],
             pK, inc=False)
        kv_ready = v_op("tensor_copy",
                        slaba[:, VS_C + tc * 512:VS_C + (tc + 1) * 512], pV)
        pair_evac[pair] = kv_ready

    qp_free = kv_ready
    pv_free = kv_ready

    def emit_qx_masks(qc, region_free_sm, ws_ids):
        ktn = KT // 2 if qc == 0 else KT
        wait("g", "sm", region_free_sm)
        g_dma(ar(QX_C, 16384), xq[:, qc * 16384:(qc + 1) * 16384])
        dq_qx = cnt["dq"]
        for h in range(2):
            wait("g", "sv", ws_ids[3])
            wait("g", "sm", ws_ids[2][h])
            ws_ids[h] = g_dma(ar(WS_C[h], 4096),
                              wq[:, h * 4096:(h + 1) * 4096], f"ws{h}")
        wait("v", "dq", d_thr)
        wait("v", "sv", d_copy)
        wait("v", "sm", region_free_sm)
        mask_g = []
        for kt in range(ktn):
            n_ = v_op("tensor_scalar", ar(MZ_C + kt * 512), D_f32[:],
                      thr_sb[:, qc * 32 + kt:qc * 32 + kt + 1], None,
                      ALU.is_ge)
            if kt % 4 == 3:
                mask_g.append(n_)
        return dq_qx, mask_g

    ws_ids = [0, 0, [0, 0], kv_ready]
    pre = {0: emit_qx_masks(0, 0, ws_ids)}
    for qc in range(n_qc):
        ktn = KT // 2 if qc == 0 else KT
        ng = ktn // 4
        qpr = DT // ng
        dq_qx, mask_g = pre[qc]

        floor_sm = cnt["sm"]
        ws_free = [max(ws_free[i], floor_sm) for i in range(2)]

        def qproj_dma(h):
            par = h % 2
            if h < 2:
                return ws_ids[h]
            wait("g", "sm", ws_free[par])
            return g_dma(ar(WS_C[par], 4096),
                         wq[:, h * 4096:(h + 1) * 4096], f"ws{par}")

        def qproj_waits(h, d_slab):
            par = h % 2
            wait("t", f"ws{par}", d_slab)
            wait("t", "dq", dq_qx)
            wait("t", "sv", qp_free)

        def qproj_mm(h, d):
            par = h % 2
            return t_mm(P7[:], ar(WS_C[par] + d * 128, 128),
                        ar(QX_C + d * 512), d == 0, d == DT - 1,
                        inc=(d == DT - 1))

        d_slab = qproj_dma(0)
        qproj_waits(0, d_slab)
        for d in range(DT):
            qn = qproj_mm(0, d)
        ws_free[0] = qn
        wait("v", "sm", qn)
        qT_ready = v_op("tensor_copy", ar(QT_C[0]), P7[:])
        qp_free = qT_ready

        epi = None
        for h in range(n_heads):
            wait("t", "sv", max(qT_ready, kv_ready))
            if h + 1 < n_heads:
                d_slab = qproj_dma(h + 1)
            d_mm = 0
            e_done = {}
            m_done = {}
            for step in range(ng + 1):
                g = step
                if g < ng:
                    wait("t", "ss", s_free)
                    for i in range(4):
                        kt = 4 * g + i
                        s_mm = t_mm(
                            BK(i),
                            slaba[:, KTS_C + kt * 128:KTS_C + (kt + 1) * 128],
                            ar(QT_C[h % 2]), True, True, inc=(i == 3))
                    wait("s", "sm", max(s_mm, pm_free[g % 2]))
                    e_done[g] = s_act(ar(PM_C[g % 2], 2048), PS4[:, 0:2048],
                                      AF.Exp, scale=SCALE)
                    s_free = e_done[g]
                    wait("v", "ss", e_done[g])
                    wait("v", "sv", mask_g[g])
                    m_done[g] = v_op("tensor_mul", ar(PM_C[g % 2], 2048),
                                     ar(PM_C[g % 2], 2048),
                                     ar(MZ_C + 4 * g * 512, 2048))
                if step == 0 and epi is not None:
                    r_done, hprev = epi
                    wait("t", "ss", max(r_done, bc_free))
                    b_mm = t_mm(P6[:], ones_rb[:], rowAb[:], True, True)
                    wait("s", "sm", b_mm)
                    c_done = s_copy(tmpf[:], P6[:])
                    bc_free = c_done
                    wait("v", "ss", c_done)
                    a_done = v_op("tensor_mul", ar(ARW_C + hprev * 512),
                                  P4[:], tmpf[:])
                    pv_free = a_done
                    epi = None
                if h + 1 < n_heads and step * qpr < DT:
                    if step == 0:
                        qproj_waits(h + 1, d_slab)
                    for dd_ in range(qpr):
                        qn = qproj_mm(h + 1, step * qpr + dd_)
                    if (step + 1) * qpr >= DT:
                        ws_free[(h + 1) % 2] = qn
                gp = step - 1
                if gp >= 0:
                    wait("t", "sv", m_done[gp])
                    if gp == 0:
                        wait("t", "sv", max(pv_free, den_free_sv))
                        wait("t", "ss", den_free)
                    for i in range(4):
                        kt = 4 * gp + i
                        t_mm(P4[:],
                             slaba[:, VS_C + kt * 128:VS_C + (kt + 1) * 128],
                             ar(PM_C[gp % 2] + i * 512),
                             kt == 0, kt == ktn - 1, inc=False)
                    for i in range(4):
                        kt = 4 * gp + i
                        if i % 2 == 0:
                            d_mm = t_mm(P5[0:1, :], ones_c[:],
                                        ar(PM_C[gp % 2] + i * 512),
                                        kt == 0, kt >= ktn - 2,
                                        inc=(i == 3), tp=(0, 0))
                        else:
                            if kt == 1:
                                wait("t", "ss", bc_free)
                            d_mm = t_mm(P6[64:65, :], ones_c[:],
                                        ar(PM_C[gp % 2] + i * 512),
                                        kt == 1, kt >= ktn - 2,
                                        inc=(i == 3), tp=(0, 64))
                    pm_free[gp % 2] = d_mm
            wait("v", "sm", d_mm)
            v1 = v_op("tensor_copy", rowB[:], P6[64:65, :])
            wait("v", "sv", v1)
            v2 = v_op("tensor_add", rowB[:], rowB[:], P5[0:1, :])
            wait("s", "sv", v2)
            lns = s_act(rowC[:], rowB[:], AF.Ln)
            wait("s", "ss", lns)
            r_done = s_act(rowAb[:], rowC[:], AF.Exp, scale=-1.0)
            den_free = r_done
            den_free_sv = max(den_free_sv, v2)
            epi = (r_done, h)
            if h + 1 < n_heads:
                wait("v", "sm", qn)
                qT_ready = v_op("tensor_copy", ar(QT_C[(h + 1) % 2]), P7[:])
                qp_free = qT_ready
        r_done, hprev = epi
        wait("t", "ss", max(r_done, bc_free))
        b_mm = t_mm(P6[:], ones_rb[:], rowAb[:], True, True)
        wait("s", "sm", b_mm)
        c_done = s_copy(tmpf[:], P6[:])
        bc_free = c_done
        wait("v", "ss", c_done)
        a_done = v_op("tensor_mul", ar(ARW_C + hprev * 512), P4[:], tmpf[:])
        pv_free = a_done
        attn_done = a_done

        z_done = 0
        on = 0
        for d2 in range(DT):
            par = d2 % 2
            wait("g", "sm", ws_free[par])
            d_slab = g_dma(ar(WS_C[par], 4096),
                           wo[:, d2 * 4096:(d2 + 1) * 4096], f"ws{par}")
            pO = Px[2 + par]
            wait("t", f"ws{par}", d_slab)
            wait("t", "sv", max(attn_done, o_free[par]))
            for d in range(DT):
                on = t_mm(pO[:], ar(WS_C[par] + d * 128, 128),
                          ar(ARW_C + d * 512), d == 0, d == DT - 1,
                          inc=(d == DT - 1))
            ws_free[par] = on
            wait("v", "sm", on)
            z_done = v_op("tensor_add", ar(MZ_C + d2 * 512),
                          ar(QX_C + d2 * 512), pO[:])
            o_free[par] = z_done
        z_all = z_done

        wait("t", "sv", z_all)

        def w_y1(d2):
            sv_ = v_op("tensor_sub", ar(TM_C), ar(MZ_C + d2 * 512), ar(MB_C))
            wait("v", "sv", sv_)
            return v_op("tensor_mul", y1sb[:, d2 * 512:(d2 + 1) * 512],
                        ar(TM_C), ar(RB_C))
        y1_all, mb1, rb1 = ln_pass(lambda d2: ar(MZ_C + d2 * 512), w_y1)

        wait("t", "sv", y1_all)
        wait("t", "dq", dq_qx)
        xa_mm = 0
        for d in range(DT):
            t_mm(BK(2, 16), gaT_sb[:, d * RK:(d + 1) * RK],
                 y1sb[:, d * 512:(d + 1) * 512], d == 0, d == DT - 1,
                 inc=False, tp=(0, 0))
            xa_mm = t_mm(PS4[32:48, 1536:2048], uaT_sb[:, d * RK:(d + 1) * RK],
                         y1sb[:, d * 512:(d + 1) * 512], d == 0, d == DT - 1,
                         inc=(d == DT - 1), tp=(0, 32))
        wait("v", "sm", xa_mm)
        v_op("tensor_copy", xag_sb[:], BK(2, 16), inc=False)
        xa_done = v_op("tensor_copy", xau_sb[:], PS4[32:48, 1536:2048])

        floor_sm = cnt["sm"]
        gu_slab_free = [max(gu_slab_free[i], floor_sm) for i in range(2)]
        gu_psum_free = [max(gu_psum_free[i], s_free) for i in range(2)]
        hid_mm = 0
        hv = 0
        hv_done = {}
        for ht in range(n_ht):
            j = ht % 2
            if ht < 2:
                wait("g", "sv", y1_all)
            wait("g", "sm", gu_slab_free[j])
            d_g = g_dma(ar(GS_C[j], 4096),
                        gwT[:, ht * 4096:(ht + 1) * 4096], f"gs{j}")
            d_u = g_dma(ar(US_C[j], 4096),
                        uwT[:, ht * 4096:(ht + 1) * 4096], f"us{j}")
            wait("g", "sm", bt_free[j])
            d_gb = g_dma(gb_sb[:, j * 128:(j + 1) * 128],
                         gbT[:, ht * 128:(ht + 1) * 128], f"gb{j}")
            d_ub = g_dma(ub_sb[:, j * 128:(j + 1) * 128],
                         ubT[:, ht * 128:(ht + 1) * 128], f"ub{j}")
            pG, pU = BK(j), BK(2 + j)
            wait("t", f"gs{j}", d_g)
            wait("t", f"us{j}", d_u)
            wait("t", f"gb{j}", d_gb)
            wait("t", f"ub{j}", d_ub)
            wait("t", "ss", gu_psum_free[j])
            wait("t", "sv", up_psum_free[j])
            if ht < 2:
                wait("t", "sv", xa_done)
            for d in range(DT):
                t_mm(pG, ar(GS_C[j] + d * 128, 128),
                     y1sb[:, d * 512:(d + 1) * 512], d == 0, False, inc=False)
                t_mm(pU, ar(US_C[j] + d * 128, 128),
                     y1sb[:, d * 512:(d + 1) * 512], d == 0, False, inc=False)
            g_mm = t_mm(pG, gb_sb[:, j * 128:(j + 1) * 128], xag_sb[:],
                        False, True)
            u_mm = t_mm(pU, ub_sb[:, j * 128:(j + 1) * 128], xau_sb[:],
                        False, True)
            bt_free[j] = u_mm
            gu_slab_free[j] = u_mm
            wait("s", "sm", g_mm)
            wait("s", "sv", sil_free[j])
            si = s_act(ar(SIL_C[j]), pG, AF.Silu)
            gu_psum_free[j] = si
            wait("v", "sm", u_mm)
            wait("v", "ss", si)
            hv = v_op("tensor_mul", ar(HID_C + ht * 512), ar(SIL_C[j]), pU)
            sil_free[j] = hv
            up_psum_free[j] = hv
            hv_done[ht] = hv
            if ht >= 2:
                hx = ht - 2
                wait("t", "sv", hv_done[hx])
                hid_mm = t_mm(P4[0:16, :], daT_sb[:, hx * RK:(hx + 1) * RK],
                              ar(HID_C + hx * 512), hx == 0, False)
        for hx in range(n_ht - 2, n_ht):
            wait("t", "sv", hv_done[hx])
            hid_mm = t_mm(P4[0:16, :], daT_sb[:, hx * RK:(hx + 1) * RK],
                          ar(HID_C + hx * 512), hx == 0, hx == n_ht - 1)
        wait("v", "sm", hid_mm)
        xad_done = v_op("tensor_copy", xad_sb[:], P4[0:16, :])

        floor_sm = cnt["sm"]
        ds_free = [max(ds_free[i], floor_sm) for i in range(4)]
        floor_sv = cnt["sv"]
        bank_free = [max(bank_free[b], floor_sv) for b in range(8)]
        z2_done = 0
        for gpass in range(4):
            wait("g", "sm", db_free)
            d_db = g_dma(db_sb[:, :], dbT[:, gpass * 1024:(gpass + 1) * 1024],
                         "db0")
            for ht in range(n_ht):
                sl = ht % 4
                wait("g", "sm", ds_free[sl])
                dd = g_dma(ar(DS_C[sl], 1024),
                           dwT[ht * 128:(ht + 1) * 128,
                               gpass * 1024:(gpass + 1) * 1024], f"ds{sl}")
                wait("t", f"ds{sl}", dd)
                if ht == 0:
                    wait("t", "ss", max(gu_psum_free))
                    wait("t", "sv", max(max(up_psum_free), xad_done))
                    for b in range(8):
                        wait("t", "sv", bank_free[b])
                dmm = 0
                for b in range(8):
                    dmm = t_mm(BK(b), ar(DS_C[sl] + b * 128, 128),
                               ar(HID_C + ht * 512), ht == 0, False,
                               inc=(b == 7))
                ds_free[sl] = dmm
            wait("t", "db0", d_db)
            wait("t", "sv", xad_done)
            lmm = 0
            for b in range(8):
                lmm = t_mm(BK(b), db_sb[:, b * 128:(b + 1) * 128],
                           xad_sb[:], False, True, inc=(b == 7))
            db_free = lmm
            wait("v", "sm", lmm)
            for b in range(8):
                d2 = gpass * 8 + b
                z2_done = v_op("tensor_add", y1sb[:, d2 * 512:(d2 + 1) * 512],
                               y1sb[:, d2 * 512:(d2 + 1) * 512], BK(b))
                bank_free[b] = z2_done
        z2_all = z2_done
        post_down_sm = cnt["sm"]
        qp_free = max(qp_free, bank_free[7])

        def hoist_next():
            if qc + 1 < n_qc:
                ws_ids[2] = [max(ws_free[i], max(gu_slab_free))
                             for i in range(2)]
                pre[qc + 1] = emit_qx_masks(qc + 1, post_down_sm, ws_ids)

        wait("t", "sv", z2_all)

        def w_out(d2):
            par = d2 % 2
            wait("v", f"so{par}", ob_free[par])
            sv_ = v_op("tensor_sub", ar(TM_C),
                       y1sb[:, d2 * 512:(d2 + 1) * 512], ar(MB_C))
            wait("v", "sv", sv_)
            ov = v_op("tensor_mul", outb[par][:], ar(TM_C), ar(RB_C))
            wait("y", "sv", ov)
            od = y_dma(out[d2 * 128:(d2 + 1) * 128,
                           qc * 512:(qc + 1) * 512], outb[par][:], f"so{par}")
            ob_free[par] = od
            return ov
        _, mb2, rb2 = ln_pass(lambda d2: y1sb[:, d2 * 512:(d2 + 1) * 512],
                              w_out, mid_cb=hoist_next)
        pv_free = max(pv_free, mb2)
        den_free_sv = rb2

        fm, fv, fs = cnt["sm"], cnt["sv"], cnt["ss"]
        ws_free = [max(ws_free[i], max(gu_slab_free)) for i in range(2)]
        gu_slab_free = [max(w, fm) for w in gu_slab_free]
        bt_free = [max(w, fm) for w in bt_free]
        ds_free = [max(w, fm) for w in ds_free]
        db_free = max(db_free, fm)
        pm_free = [max(w, fm) for w in pm_free]
        s_free = max(s_free, fs)
        gu_psum_free = [max(w, fs) for w in gu_psum_free]
        bc_free = max(bc_free, fs)
        den_free = max(den_free, fs)
        up_psum_free = [max(w, fv) for w in up_psum_free]
        sil_free = [max(w, fv) for w in sil_free]
        bank_free = [max(w, fv) for w in bank_free]
        o_free = [max(w, fv) for w in o_free]

    wait("y", "so0", cnt["so0"])
    wait("y", "so1", cnt["so1"])

    with nc.Block() as block:
        @block.tensor
        def _(t):
            for fn in prog["t"]:
                fn(t)

        @block.vector
        def _(v):
            for fn in prog["v"]:
                fn(v)

        @block.scalar
        def _(s):
            for fn in prog["s"]:
                fn(s)

        @block.gpsimd
        def _(g):
            for fn in prog["g"]:
                fn(g)

        @block.sync
        def _(y):
            for fn in prog["y"]:
                fn(y)

    return nc


def kernel(L0_attn_x, L0_attn_wq, L0_attn_wk, L0_attn_wv, L0_attn_wo,
           L0_ffn_gate_w, L0_ffn_up_w, L0_ffn_down_w,
           L0_ffn_gate_a, L0_ffn_gate_b, L0_ffn_up_a, L0_ffn_up_b,
           L0_ffn_down_a, L0_ffn_down_b):
    f = np.float32
    x = np.asarray(L0_attn_x, dtype=f)
    B = x.shape[0]
    xT = [np.ascontiguousarray(x[b].T) for b in range(B)]

    def tile_di(w):
        d_in, cols = w.shape
        a, ct = d_in // 128, cols // 128
        return np.ascontiguousarray(
            w.reshape(a, 128, ct, 128).transpose(1, 2, 0, 3).reshape(128, -1))

    def tile_rows(w):
        d_in, c = w.shape
        return np.ascontiguousarray(
            w.reshape(d_in // 128, 128, c).transpose(1, 0, 2).reshape(128, -1))

    com = {
        "wq": tile_di(np.asarray(L0_attn_wq, f)),
        "wk": tile_rows(np.asarray(L0_attn_wk, f)),
        "wv": tile_rows(np.asarray(L0_attn_wv, f)),
        "wo": tile_di(np.asarray(L0_attn_wo, f)),
        "gwT": tile_di(np.ascontiguousarray(np.asarray(L0_ffn_gate_w, f).T)),
        "uwT": tile_di(np.ascontiguousarray(np.asarray(L0_ffn_up_w, f).T)),
        "dwT": np.ascontiguousarray(np.asarray(L0_ffn_down_w, f).T),
        "gaT": tile_rows(np.ascontiguousarray(np.asarray(L0_ffn_gate_a, f).T)),
        "gbT": np.ascontiguousarray(np.asarray(L0_ffn_gate_b, f).T),
        "uaT": tile_rows(np.ascontiguousarray(np.asarray(L0_ffn_up_a, f).T)),
        "ubT": np.ascontiguousarray(np.asarray(L0_ffn_up_b, f).T),
        "daT": tile_rows(np.ascontiguousarray(np.asarray(L0_ffn_down_a, f).T)),
        "dbT": np.ascontiguousarray(np.asarray(L0_ffn_down_b, f).T),
    }
    in_maps = []
    for c in range(8):
        b, q4 = c // 4, c % 4
        thr = np.empty((128, 64), dtype=f)
        for qc in range(2):
            for kt in range(32):
                thr[:, qc * 32 + kt] = kt * 128 - 2048 * qc - q4
        m = dict(com)
        m["xT"] = xT[b]
        xqc = xT[b][:, q4::4]
        m["xq"] = np.ascontiguousarray(
            xqc.reshape(32, 128, 2, 512).transpose(1, 2, 0, 3).reshape(128, -1))
        m["thr"] = thr
        in_maps.append(m)

    nc = build_program()
    res = run_bass_kernel_spmd(nc, in_maps, core_ids=list(range(8)))
    outp = np.empty((B, T, DM), dtype=f)
    for c in range(8):
        b, q4 = c // 4, c % 4
        outp[b, q4::4, :] = res.results[c]["out"].T
    return outp


# version 20
# speedup vs baseline: 1.0462x; 1.0462x over previous
import numpy as np
import concourse.bass as bass
import concourse.mybir as mybir
from concourse.bass_utils import run_bass_kernel_spmd

F32 = mybir.dt.float32
BF16 = mybir.dt.bfloat16
I32 = mybir.dt.int32
AF = mybir.ActivationFunctionType
ALU = mybir.AluOpType

DM = 4096
T = 4096
NH = 32
HD = 128
HFF = 11008
HT = HFF // 128
DT = DM // 128
KT = T // 128
TC = T // 512
NQC = 2
RK = 16
SCALE = 1.0 / float(np.sqrt(HD))
INV_D = 1.0 / DM
EPS = 1e-5

AR_COLS = 64512
QX_C = 0
MZ_C = 16384
ARW_C = 32768
TR_C = 49152
WK_C = TR_C
WV_C = TR_C + 4096
XKV_C = TR_C + 8192
WS_C = [TR_C, TR_C + 4096]
QT_C = [TR_C + 8192, TR_C + 10240]
MB_C = TR_C + 8704
RB_C = TR_C + 9216
TM_C = TR_C + 9728
PM_C = [TR_C + 10752, TR_C + 12800]
ZQ_C = TR_C + 14848
HID_C = 0
GS_C = [44032, 48128]
US_C = [52224, 56320]
SIL_C = [60416, 60928]
DS_C = [44032 + 1024 * i for i in range(4)]
KTS_C = 0
VS_C = 4096


def build_program(n_heads=NH, n_qc=NQC, n_tc=TC, n_ht=HT):
    nc = bass.Bass("TRN2", target_bir_lowering=False)

    xT = nc.dram_tensor("xT", [DM, T], F32, kind="ExternalInput").ap()
    xq = nc.dram_tensor("xq", [128, 2 * DT * 512], F32,
                        kind="ExternalInput").ap()
    wq = nc.dram_tensor("wq", [128, NH * DM], F32, kind="ExternalInput").ap()
    wk = nc.dram_tensor("wk", [128, DT * HD], F32, kind="ExternalInput").ap()
    wv = nc.dram_tensor("wv", [128, DT * HD], F32, kind="ExternalInput").ap()
    wo = nc.dram_tensor("wo", [128, DT * DM], F32, kind="ExternalInput").ap()
    gwT = nc.dram_tensor("gwT", [128, HT * DM], F32, kind="ExternalInput").ap()
    uwT = nc.dram_tensor("uwT", [128, HT * DM], F32, kind="ExternalInput").ap()
    dwT = nc.dram_tensor("dwT", [HFF, DM], F32, kind="ExternalInput").ap()
    gaT = nc.dram_tensor("gaT", [128, DT * RK], F32, kind="ExternalInput").ap()
    gbT = nc.dram_tensor("gbT", [RK, HFF], F32, kind="ExternalInput").ap()
    uaT = nc.dram_tensor("uaT", [128, DT * RK], F32, kind="ExternalInput").ap()
    ubT = nc.dram_tensor("ubT", [RK, HFF], F32, kind="ExternalInput").ap()
    daT = nc.dram_tensor("daT", [128, HT * RK], F32, kind="ExternalInput").ap()
    dbT = nc.dram_tensor("dbT", [RK, DM], F32, kind="ExternalInput").ap()
    thr = nc.dram_tensor("thr", [128, 64], F32, kind="ExternalInput").ap()
    out = nc.dram_tensor("out", [DM, 1024], F32, kind="ExternalOutput").ap()

    arena = nc.alloc_sbuf_tensor("arena", [128, AR_COLS], BF16)
    slaba = nc.alloc_sbuf_tensor("slaba", [128, 8192], BF16)
    y1sb = nc.alloc_sbuf_tensor("y1sb", [128, 16384], BF16)
    D_i32 = nc.alloc_sbuf_tensor("D_i32", [128, 512], I32)
    D_f32 = nc.alloc_sbuf_tensor("D_f32", [128, 512], F32)
    thr_sb = nc.alloc_sbuf_tensor("thr_sb", [128, 64], F32)
    ones_c = nc.alloc_sbuf_tensor("ones_c", [128, 1], BF16)
    ones_r = nc.alloc_sbuf_tensor("ones_r", [1, 128], F32)
    gaT_sb = nc.alloc_sbuf_tensor("gaT_sb", [128, DT * RK], BF16)
    uaT_sb = nc.alloc_sbuf_tensor("uaT_sb", [128, DT * RK], BF16)
    daT_sb = nc.alloc_sbuf_tensor("daT_sb", [128, HT * RK], BF16)
    gb_sb = nc.alloc_sbuf_tensor("gb_sb", [16, 2 * 128], BF16)
    ub_sb = nc.alloc_sbuf_tensor("ub_sb", [16, 2 * 128], BF16)
    db_sb = nc.alloc_sbuf_tensor("db_sb", [16, 1024], BF16)
    xag_sb = nc.alloc_sbuf_tensor("xag_sb", [16, 512], BF16)
    xau_sb = nc.alloc_sbuf_tensor("xau_sb", [16, 512], BF16)
    xad_sb = nc.alloc_sbuf_tensor("xad_sb", [16, 512], BF16)
    rowA = nc.alloc_sbuf_tensor("rowA", [1, 512], F32)
    rowAb = nc.alloc_sbuf_tensor("rowAb", [1, 512], BF16)
    ones_rb = nc.alloc_sbuf_tensor("ones_rb", [1, 128], BF16)
    rowB = nc.alloc_sbuf_tensor("rowB", [1, 512], F32)
    rowC = nc.alloc_sbuf_tensor("rowC", [1, 512], F32)
    tmpf = nc.alloc_sbuf_tensor("tmpf", [128, 512], F32)
    outb = [nc.alloc_sbuf_tensor(f"outb{i}", [128, 512], F32) for i in range(2)]

    PS4 = nc.alloc_psum_tensor("ps0123", [128, 2048], F32)
    P4 = nc.alloc_psum_tensor("ps4", [128, 512], F32)
    P5 = nc.alloc_psum_tensor("ps5", [128, 512], F32)
    P6 = nc.alloc_psum_tensor("ps6", [128, 512], F32)
    P7 = nc.alloc_psum_tensor("ps7", [128, 512], F32)
    Px = [P4, P5, P6, P7]

    def BK(i, r=128):
        if i < 4:
            return PS4[0:r, i * 512:(i + 1) * 512]
        return Px[i - 4][0:r, :]

    sems = {}
    for name in ["dq", "sm", "sv", "ss", "sg",
                 "xk0", "xk1", "xk2", "xk3", "xk4", "xk5", "xk6", "xk7",
                 "ws0", "ws1",
                 "gs0", "gs1", "us0", "us1", "gb0", "gb1", "ub0", "ub1",
                 "ds0", "ds1", "ds2", "ds3", "db0", "so0", "so1"]:
        sems[name] = nc.alloc_semaphore(name)
    dq, sm, sv, ss, sg = (sems[k] for k in ["dq", "sm", "sv", "ss", "sg"])

    cnt = {k: 0 for k in sems}
    waited = {}
    prog = {k: [] for k in "tvsgy"}

    def emit(eng, fn):
        prog[eng].append(fn)

    def wait(eng, ch, val):
        if not val or val <= 0:
            return
        key = (eng, ch)
        if waited.get(key, 0) >= val:
            return
        waited[key] = val
        s = sems[ch]
        emit(eng, lambda e, s=s, v=val: e.wait_ge(s, v))

    def bump(ch, k=1):
        cnt[ch] += k
        return cnt[ch]

    def g_dma(dst, src, ch="dq"):
        n_ = bump(ch, 16)
        sem = sems[ch]
        emit("g", lambda g, d=dst, s=src, se=sem:
             g.dma_start(d, s).then_inc(se, 16))
        return n_

    def y_dma(dst, src, ch):
        n_ = bump(ch, 16)
        sem = sems[ch]
        emit("y", lambda y, d=dst, s=src, se=sem:
             y.dma_start(d, s).then_inc(se, 16))
        return n_

    def t_mm(o, l, r, st, sp, inc=True, tp=None):
        if inc:
            n_ = bump("sm")
            emit("t", lambda t, o=o, l=l, r=r, a=st, b=sp, c=tp:
                 t.matmul(o, l, r, start=a, stop=b,
                          tile_position=c).then_inc(sm, 1))
            return n_
        emit("t", lambda t, o=o, l=l, r=r, a=st, b=sp, c=tp:
             t.matmul(o, l, r, start=a, stop=b, tile_position=c))
        return None

    def v_op(name, *aps, inc=True):
        if inc:
            n_ = bump("sv")
            emit("v", lambda v, nm=name, a=aps: getattr(v, nm)(*a).then_inc(sv, 1))
            return n_
        emit("v", lambda v, nm=name, a=aps: getattr(v, nm)(*a))
        return None

    def s_act(o, i, func, scale=1.0, inc=True):
        if inc:
            n_ = bump("ss")
            emit("s", lambda s, o=o, i=i, f=func, sc=scale:
                 s.activation(o, i, f, scale=sc).then_inc(ss, 1))
            return n_
        emit("s", lambda s, o=o, i=i, f=func, sc=scale:
             s.activation(o, i, f, scale=sc))
        return None

    def s_copy(o, i):
        n_ = bump("ss")
        emit("s", lambda s, o=o, i=i: s.copy(o, i).then_inc(ss, 1))
        return n_

    def ar(c0, w=512):
        return arena[:, c0:c0 + w]

    def ln_pass(zt, out_write, mid_cb=None):
        st_mm = 0
        zq_free = [0, 0]
        zq_c = [ZQ_C, SIL_C[0]]
        sq = PS4[64:65, 1536:2048]
        for d2 in range(DT):
            jz = d2 % 2
            wait("v", "sm", zq_free[jz])
            zq = v_op("tensor_mul", ar(zq_c[jz]), zt(d2), zt(d2))
            t_mm(BK(2, 1), ones_c[:], zt(d2), d2 == 0, d2 == DT - 1,
                 inc=False, tp=(0, 0))
            wait("t", "sv", zq)
            st_mm = t_mm(sq, ones_c[:], ar(zq_c[jz]),
                         d2 == 0, d2 == DT - 1, tp=(0, 64))
            zq_free[jz] = st_mm
        if mid_cb is not None:
            mid_cb()
        wait("v", "sm", st_mm)
        va = v_op("tensor_scalar", rowA[:], BK(2, 1), INV_D, None, ALU.mult)
        vb = v_op("tensor_scalar", rowB[:], sq, INV_D, None, ALU.mult)
        wait("v", "sv", va)
        vc = v_op("tensor_mul", rowC[:], rowA[:], rowA[:])
        wait("v", "sv", max(vb, vc))
        vd = v_op("tensor_sub", rowB[:], rowB[:], rowC[:])
        wait("v", "sv", vd)
        rv = v_op("tensor_scalar", rowB[:], rowB[:], EPS, None, ALU.add)
        wait("s", "sv", rv)
        ln1 = s_act(rowC[:], rowB[:], AF.Ln)
        wait("s", "ss", ln1)
        rr = s_act(rowB[:], rowC[:], AF.Exp, scale=-0.5)
        wait("t", "sv", rv)
        wait("t", "ss", rr)
        t_mm(P4[:], ones_r[:], rowA[:], True, True, inc=False)
        bb = t_mm(P5[:], ones_r[:], rowB[:], True, True)
        wait("v", "sm", bb)
        p4f = v_op("tensor_copy", ar(MB_C), P4[:])
        p5f = v_op("tensor_copy", ar(RB_C), P5[:])
        last = 0
        for d2 in range(DT):
            last = out_write(d2)
        return last, p4f, p5f

    xkv_free = [0, 0, 0]
    pair_evac = [0, 0]
    ws_free = [0, 0]
    qp_free = 0
    s_free = 0
    pm_free = [0, 0]
    pv_free = 0
    den_free = 0
    den_free_sv = 0
    bc_free = 0
    o_free = [0, 0]
    gu_slab_free = [0, 0]
    gu_psum_free = [0, 0]
    up_psum_free = [0, 0]
    sil_free = [0, 0]
    bt_free = [0, 0]
    ds_free = [0, 0, 0, 0]
    db_free = 0
    bank_free = [0] * 8
    ob_free = [0, 0]
    qx_free = 0
    mz_free = 0

    bump("sg")
    emit("g", lambda g: g.iota(D_i32[:], [[4, 512]], base=0,
                               channel_multiplier=-1).then_inc(sg, 1))
    bump("sg")
    emit("g", lambda g: g.memset(ones_c[:], 1.0).then_inc(sg, 1))
    bump("sg")
    emit("g", lambda g: g.memset(ones_r[:], 1.0).then_inc(sg, 1))
    bump("sg")
    emit("g", lambda g: g.memset(ones_rb[:], 1.0).then_inc(sg, 1))
    sg_setup = cnt["sg"]

    g_dma(thr_sb[:], thr[:, :])
    g_dma(arena[:, WK_C:WK_C + 4096], wk[:, :])
    g_dma(arena[:, WV_C:WV_C + 4096], wv[:, :])
    g_dma(gaT_sb[:], gaT[:, :])
    g_dma(uaT_sb[:], uaT[:, :])
    g_dma(daT_sb[:], daT[:, :])
    d_thr = cnt["dq"]

    wait("v", "sg", sg_setup)
    d_copy = v_op("tensor_copy", D_f32[:], D_i32[:])

    kv_ready = 0
    for tc in range(n_tc):
        pair = tc % 2
        pK, pV = BK(pair * 2), BK(pair * 2 + 1)
        wait("t", "dq", cnt["dq"])
        wait("t", "sg", sg_setup)
        first = True
        for dg in range(DT // 4):
            slot = (tc * (DT // 4) + dg) % 3
            wait("g", "sm", xkv_free[slot])
            src = xT[dg * 512:(dg + 1) * 512,
                     tc * 512:(tc + 1) * 512].rearrange(
                         "(a p) c -> p a c", p=128)
            dd = g_dma(ar(XKV_C + slot * 2048,
                          2048).rearrange("p (a c) -> p a c", c=512),
                       src, f"xk{slot}")
            wait("t", f"xk{slot}", dd)
            if first:
                wait("t", "sv", pair_evac[pair])
                first = False
            for di in range(4):
                d = dg * 4 + di
                xkv = ar(XKV_C + slot * 2048 + di * 512)
                t_mm(pK, arena[:, WK_C + d * 128:WK_C + (d + 1) * 128], xkv,
                     d == 0, d == DT - 1, inc=False)
                for tt in range(4):
                    mmn = t_mm(pV[:, tt * 128:(tt + 1) * 128],
                               xkv[:, tt * 128:(tt + 1) * 128],
                               arena[:, WV_C + d * 128:WV_C + (d + 1) * 128],
                               (d == 0 and tt == 0),
                               (d == DT - 1 and tt == 3), inc=(tt == 3))
            xkv_free[slot] = mmn
        wait("v", "sm", mmn)
        v_op("tensor_copy", slaba[:, KTS_C + tc * 512:KTS_C + (tc + 1) * 512],
             pK, inc=False)
        kv_ready = v_op("tensor_copy",
                        slaba[:, VS_C + tc * 512:VS_C + (tc + 1) * 512], pV)
        pair_evac[pair] = kv_ready

    qp_free = kv_ready
    pv_free = kv_ready

    def emit_qx_masks(qc, region_free_sm, ws_ids):
        ktn = KT // 2 if qc == 0 else KT
        wait("g", "sm", region_free_sm)
        g_dma(ar(QX_C, 16384), xq[:, qc * 16384:(qc + 1) * 16384])
        dq_qx = cnt["dq"]
        for h in range(2):
            wait("g", "sv", ws_ids[3])
            wait("g", "sm", ws_ids[2][h])
            ws_ids[h] = g_dma(ar(WS_C[h], 4096),
                              wq[:, h * 4096:(h + 1) * 4096], f"ws{h}")
        wait("v", "dq", d_thr)
        wait("v", "sv", d_copy)
        wait("v", "sm", region_free_sm)
        mask_g = []
        for kt in range(ktn):
            n_ = v_op("tensor_scalar", ar(MZ_C + kt * 512), D_f32[:],
                      thr_sb[:, qc * 32 + kt:qc * 32 + kt + 1], None,
                      ALU.is_ge)
            if kt % 4 == 3:
                mask_g.append(n_)
        return dq_qx, mask_g

    ws_ids = [0, 0, [0, 0], kv_ready]
    pre = {0: emit_qx_masks(0, 0, ws_ids)}
    for qc in range(n_qc):
        ktn = KT // 2 if qc == 0 else KT
        ng = ktn // 4
        qpr = DT // ng
        dq_qx, mask_g = pre[qc]

        floor_sm = cnt["sm"]
        ws_free = [max(ws_free[i], floor_sm) for i in range(2)]

        def qproj_dma(h):
            par = h % 2
            if h < 2:
                return ws_ids[h]
            wait("g", "sm", ws_free[par])
            return g_dma(ar(WS_C[par], 4096),
                         wq[:, h * 4096:(h + 1) * 4096], f"ws{par}")

        def qproj_waits(h, d_slab):
            par = h % 2
            wait("t", f"ws{par}", d_slab)
            wait("t", "dq", dq_qx)
            wait("t", "sv", qp_free)

        def qproj_mm(h, d):
            par = h % 2
            return t_mm(P7[:], ar(WS_C[par] + d * 128, 128),
                        ar(QX_C + d * 512), d == 0, d == DT - 1,
                        inc=(d == DT - 1))

        d_slab = qproj_dma(0)
        qproj_waits(0, d_slab)
        for d in range(DT):
            qn = qproj_mm(0, d)
        ws_free[0] = qn
        wait("v", "sm", qn)
        qT_ready = v_op("tensor_copy", ar(QT_C[0]), P7[:])
        qp_free = qT_ready

        epi = None
        for h in range(n_heads):
            wait("t", "sv", max(qT_ready, kv_ready))
            if h + 1 < n_heads:
                d_slab = qproj_dma(h + 1)
            d_mm = 0
            e_done = {}
            m_done = {}
            for step in range(ng + 1):
                g = step
                if g < ng:
                    wait("t", "ss", s_free)
                    for i in range(4):
                        kt = 4 * g + i
                        s_mm = t_mm(
                            BK(i),
                            slaba[:, KTS_C + kt * 128:KTS_C + (kt + 1) * 128],
                            ar(QT_C[h % 2]), True, True, inc=(i == 3))
                    wait("s", "sm", max(s_mm, pm_free[g % 2]))
                    e_done[g] = s_act(ar(PM_C[g % 2], 2048), PS4[:, 0:2048],
                                      AF.Exp, scale=SCALE)
                    s_free = e_done[g]
                    wait("v", "ss", e_done[g])
                    wait("v", "sv", mask_g[g])
                    m_done[g] = v_op("tensor_mul", ar(PM_C[g % 2], 2048),
                                     ar(PM_C[g % 2], 2048),
                                     ar(MZ_C + 4 * g * 512, 2048))
                if step == 0 and epi is not None:
                    r_done, hprev = epi
                    wait("t", "ss", max(r_done, bc_free))
                    b_mm = t_mm(P6[:], ones_rb[:], rowAb[:], True, True)
                    wait("s", "sm", b_mm)
                    c_done = s_copy(tmpf[:], P6[:])
                    bc_free = c_done
                    wait("v", "ss", c_done)
                    a_done = v_op("tensor_mul", ar(ARW_C + hprev * 512),
                                  P4[:], tmpf[:])
                    pv_free = a_done
                    epi = None
                if h + 1 < n_heads and step * qpr < DT:
                    if step == 0:
                        qproj_waits(h + 1, d_slab)
                    for dd_ in range(qpr):
                        qn = qproj_mm(h + 1, step * qpr + dd_)
                    if (step + 1) * qpr >= DT:
                        ws_free[(h + 1) % 2] = qn
                gp = step - 1
                if gp >= 0:
                    wait("t", "sv", m_done[gp])
                    if gp == 0:
                        wait("t", "sv", max(pv_free, den_free_sv))
                        wait("t", "ss", den_free)
                    for i in range(4):
                        kt = 4 * gp + i
                        t_mm(P4[:],
                             slaba[:, VS_C + kt * 128:VS_C + (kt + 1) * 128],
                             ar(PM_C[gp % 2] + i * 512),
                             kt == 0, kt == ktn - 1, inc=False)
                    for i in range(4):
                        kt = 4 * gp + i
                        if i % 2 == 0:
                            d_mm = t_mm(P5[0:1, :], ones_c[:],
                                        ar(PM_C[gp % 2] + i * 512),
                                        kt == 0, kt >= ktn - 2,
                                        inc=(i == 3), tp=(0, 0))
                        else:
                            if kt == 1:
                                wait("t", "ss", bc_free)
                            d_mm = t_mm(P6[64:65, :], ones_c[:],
                                        ar(PM_C[gp % 2] + i * 512),
                                        kt == 1, kt >= ktn - 2,
                                        inc=(i == 3), tp=(0, 64))
                    pm_free[gp % 2] = d_mm
            wait("v", "sm", d_mm)
            v1 = v_op("tensor_copy", rowB[:], P6[64:65, :])
            wait("v", "sv", v1)
            v2 = v_op("tensor_add", rowB[:], rowB[:], P5[0:1, :])
            wait("s", "sv", v2)
            lns = s_act(rowC[:], rowB[:], AF.Ln)
            wait("s", "ss", lns)
            r_done = s_act(rowAb[:], rowC[:], AF.Exp, scale=-1.0)
            den_free = r_done
            den_free_sv = max(den_free_sv, v2)
            epi = (r_done, h)
            if h + 1 < n_heads:
                wait("v", "sm", qn)
                qT_ready = v_op("tensor_copy", ar(QT_C[(h + 1) % 2]), P7[:])
                qp_free = qT_ready
        r_done, hprev = epi
        wait("t", "ss", max(r_done, bc_free))
        b_mm = t_mm(P6[:], ones_rb[:], rowAb[:], True, True)
        wait("s", "sm", b_mm)
        c_done = s_copy(tmpf[:], P6[:])
        bc_free = c_done
        wait("v", "ss", c_done)
        a_done = v_op("tensor_mul", ar(ARW_C + hprev * 512), P4[:], tmpf[:])
        pv_free = a_done
        attn_done = a_done

        z_done = 0
        on = 0
        for d2 in range(DT):
            par = d2 % 2
            wait("g", "sm", ws_free[par])
            d_slab = g_dma(ar(WS_C[par], 4096),
                           wo[:, d2 * 4096:(d2 + 1) * 4096], f"ws{par}")
            pO = Px[2 + par]
            wait("t", f"ws{par}", d_slab)
            wait("t", "sv", max(attn_done, o_free[par]))
            for d in range(DT):
                on = t_mm(pO[:], ar(WS_C[par] + d * 128, 128),
                          ar(ARW_C + d * 512), d == 0, d == DT - 1,
                          inc=(d == DT - 1))
            ws_free[par] = on
            wait("v", "sm", on)
            z_done = v_op("tensor_add", ar(MZ_C + d2 * 512),
                          ar(QX_C + d2 * 512), pO[:])
            o_free[par] = z_done
        z_all = z_done

        wait("t", "sv", z_all)

        def w_y1(d2):
            sv_ = v_op("tensor_sub", ar(TM_C), ar(MZ_C + d2 * 512), ar(MB_C))
            wait("v", "sv", sv_)
            return v_op("tensor_mul", y1sb[:, d2 * 512:(d2 + 1) * 512],
                        ar(TM_C), ar(RB_C))
        y1_all, mb1, rb1 = ln_pass(lambda d2: ar(MZ_C + d2 * 512), w_y1)

        wait("t", "sv", y1_all)
        wait("t", "dq", dq_qx)
        xa_mm = 0
        for d in range(DT):
            t_mm(BK(2, 16), gaT_sb[:, d * RK:(d + 1) * RK],
                 y1sb[:, d * 512:(d + 1) * 512], d == 0, d == DT - 1,
                 inc=False, tp=(0, 0))
            xa_mm = t_mm(PS4[32:48, 1536:2048], uaT_sb[:, d * RK:(d + 1) * RK],
                         y1sb[:, d * 512:(d + 1) * 512], d == 0, d == DT - 1,
                         inc=(d == DT - 1), tp=(0, 32))
        wait("v", "sm", xa_mm)
        v_op("tensor_copy", xag_sb[:], BK(2, 16), inc=False)
        xa_done = v_op("tensor_copy", xau_sb[:], PS4[32:48, 1536:2048])

        floor_sm = cnt["sm"]
        gu_slab_free = [max(gu_slab_free[i], floor_sm) for i in range(2)]
        gu_psum_free = [max(gu_psum_free[i], s_free) for i in range(2)]
        hid_mm = 0
        hv = 0
        hv_done = {}
        for ht in range(n_ht):
            j = ht % 2
            if ht < 2:
                wait("g", "sv", y1_all)
            wait("g", "sm", gu_slab_free[j])
            d_g = g_dma(ar(GS_C[j], 4096),
                        gwT[:, ht * 4096:(ht + 1) * 4096], f"gs{j}")
            d_u = g_dma(ar(US_C[j], 4096),
                        uwT[:, ht * 4096:(ht + 1) * 4096], f"us{j}")
            wait("g", "sm", bt_free[j])
            d_gb = g_dma(gb_sb[:, j * 128:(j + 1) * 128],
                         gbT[:, ht * 128:(ht + 1) * 128], f"gb{j}")
            d_ub = g_dma(ub_sb[:, j * 128:(j + 1) * 128],
                         ubT[:, ht * 128:(ht + 1) * 128], f"ub{j}")
            pG, pU = BK(j), BK(2 + j)
            wait("t", f"gs{j}", d_g)
            wait("t", f"us{j}", d_u)
            wait("t", f"gb{j}", d_gb)
            wait("t", f"ub{j}", d_ub)
            wait("t", "ss", gu_psum_free[j])
            wait("t", "sv", up_psum_free[j])
            if ht < 2:
                wait("t", "sv", xa_done)
            for d in range(DT):
                t_mm(pG, ar(GS_C[j] + d * 128, 128),
                     y1sb[:, d * 512:(d + 1) * 512], d == 0, False, inc=False)
            for d in range(DT):
                t_mm(pU, ar(US_C[j] + d * 128, 128),
                     y1sb[:, d * 512:(d + 1) * 512], d == 0, False, inc=False)
            g_mm = t_mm(pG, gb_sb[:, j * 128:(j + 1) * 128], xag_sb[:],
                        False, True)
            u_mm = t_mm(pU, ub_sb[:, j * 128:(j + 1) * 128], xau_sb[:],
                        False, True)
            bt_free[j] = u_mm
            gu_slab_free[j] = u_mm
            wait("s", "sm", g_mm)
            wait("s", "sv", sil_free[j])
            si = s_act(ar(SIL_C[j]), pG, AF.Silu)
            gu_psum_free[j] = si
            wait("v", "sm", u_mm)
            wait("v", "ss", si)
            hv = v_op("tensor_mul", ar(HID_C + ht * 512), ar(SIL_C[j]), pU)
            sil_free[j] = hv
            up_psum_free[j] = hv
            hv_done[ht] = hv
            if ht >= 2:
                hx = ht - 2
                wait("t", "sv", hv_done[hx])
                hid_mm = t_mm(P4[0:16, :], daT_sb[:, hx * RK:(hx + 1) * RK],
                              ar(HID_C + hx * 512), hx == 0, False)
        for hx in range(n_ht - 2, n_ht):
            wait("t", "sv", hv_done[hx])
            hid_mm = t_mm(P4[0:16, :], daT_sb[:, hx * RK:(hx + 1) * RK],
                          ar(HID_C + hx * 512), hx == 0, hx == n_ht - 1)
        wait("v", "sm", hid_mm)
        xad_done = v_op("tensor_copy", xad_sb[:], P4[0:16, :])

        floor_sm = cnt["sm"]
        ds_free = [max(ds_free[i], floor_sm) for i in range(4)]
        floor_sv = cnt["sv"]
        bank_free = [max(bank_free[b], floor_sv) for b in range(8)]
        z2_done = 0
        for gpass in range(4):
            wait("g", "sm", db_free)
            d_db = g_dma(db_sb[:, :], dbT[:, gpass * 1024:(gpass + 1) * 1024],
                         "db0")
            for ht in range(n_ht):
                sl = ht % 4
                wait("g", "sm", ds_free[sl])
                dd = g_dma(ar(DS_C[sl], 1024),
                           dwT[ht * 128:(ht + 1) * 128,
                               gpass * 1024:(gpass + 1) * 1024], f"ds{sl}")
                wait("t", f"ds{sl}", dd)
                if ht == 0:
                    wait("t", "ss", max(gu_psum_free))
                    wait("t", "sv", max(max(up_psum_free), xad_done))
                    for b in range(8):
                        wait("t", "sv", bank_free[b])
                dmm = 0
                for b in range(8):
                    dmm = t_mm(BK(b), ar(DS_C[sl] + b * 128, 128),
                               ar(HID_C + ht * 512), ht == 0, False,
                               inc=(b == 7))
                ds_free[sl] = dmm
            wait("t", "db0", d_db)
            wait("t", "sv", xad_done)
            lmm = 0
            for b in range(8):
                lmm = t_mm(BK(b), db_sb[:, b * 128:(b + 1) * 128],
                           xad_sb[:], False, True, inc=(b == 7))
            db_free = lmm
            wait("v", "sm", lmm)
            for b in range(8):
                d2 = gpass * 8 + b
                z2_done = v_op("tensor_add", y1sb[:, d2 * 512:(d2 + 1) * 512],
                               y1sb[:, d2 * 512:(d2 + 1) * 512], BK(b))
                bank_free[b] = z2_done
        z2_all = z2_done
        post_down_sm = cnt["sm"]
        qp_free = max(qp_free, bank_free[7])

        def hoist_next():
            if qc + 1 < n_qc:
                ws_ids[2] = [max(ws_free[i], max(gu_slab_free))
                             for i in range(2)]
                pre[qc + 1] = emit_qx_masks(qc + 1, post_down_sm, ws_ids)

        wait("t", "sv", z2_all)

        def w_out(d2):
            par = d2 % 2
            wait("v", f"so{par}", ob_free[par])
            sv_ = v_op("tensor_sub", ar(TM_C),
                       y1sb[:, d2 * 512:(d2 + 1) * 512], ar(MB_C))
            wait("v", "sv", sv_)
            ov = v_op("tensor_mul", outb[par][:], ar(TM_C), ar(RB_C))
            wait("y", "sv", ov)
            od = y_dma(out[d2 * 128:(d2 + 1) * 128,
                           qc * 512:(qc + 1) * 512], outb[par][:], f"so{par}")
            ob_free[par] = od
            return ov
        _, mb2, rb2 = ln_pass(lambda d2: y1sb[:, d2 * 512:(d2 + 1) * 512],
                              w_out, mid_cb=hoist_next)
        pv_free = max(pv_free, mb2)
        den_free_sv = rb2

        fm, fv, fs = cnt["sm"], cnt["sv"], cnt["ss"]
        ws_free = [max(ws_free[i], max(gu_slab_free)) for i in range(2)]
        gu_slab_free = [max(w, fm) for w in gu_slab_free]
        bt_free = [max(w, fm) for w in bt_free]
        ds_free = [max(w, fm) for w in ds_free]
        db_free = max(db_free, fm)
        pm_free = [max(w, fm) for w in pm_free]
        s_free = max(s_free, fs)
        gu_psum_free = [max(w, fs) for w in gu_psum_free]
        bc_free = max(bc_free, fs)
        den_free = max(den_free, fs)
        up_psum_free = [max(w, fv) for w in up_psum_free]
        sil_free = [max(w, fv) for w in sil_free]
        bank_free = [max(w, fv) for w in bank_free]
        o_free = [max(w, fv) for w in o_free]

    wait("y", "so0", cnt["so0"])
    wait("y", "so1", cnt["so1"])

    with nc.Block() as block:
        @block.tensor
        def _(t):
            for fn in prog["t"]:
                fn(t)

        @block.vector
        def _(v):
            for fn in prog["v"]:
                fn(v)

        @block.scalar
        def _(s):
            for fn in prog["s"]:
                fn(s)

        @block.gpsimd
        def _(g):
            for fn in prog["g"]:
                fn(g)

        @block.sync
        def _(y):
            for fn in prog["y"]:
                fn(y)

    return nc


def kernel(L0_attn_x, L0_attn_wq, L0_attn_wk, L0_attn_wv, L0_attn_wo,
           L0_ffn_gate_w, L0_ffn_up_w, L0_ffn_down_w,
           L0_ffn_gate_a, L0_ffn_gate_b, L0_ffn_up_a, L0_ffn_up_b,
           L0_ffn_down_a, L0_ffn_down_b):
    f = np.float32
    x = np.asarray(L0_attn_x, dtype=f)
    B = x.shape[0]
    xT = [np.ascontiguousarray(x[b].T) for b in range(B)]

    def tile_di(w):
        d_in, cols = w.shape
        a, ct = d_in // 128, cols // 128
        return np.ascontiguousarray(
            w.reshape(a, 128, ct, 128).transpose(1, 2, 0, 3).reshape(128, -1))

    def tile_rows(w):
        d_in, c = w.shape
        return np.ascontiguousarray(
            w.reshape(d_in // 128, 128, c).transpose(1, 0, 2).reshape(128, -1))

    com = {
        "wq": tile_di(np.asarray(L0_attn_wq, f)),
        "wk": tile_rows(np.asarray(L0_attn_wk, f)),
        "wv": tile_rows(np.asarray(L0_attn_wv, f)),
        "wo": tile_di(np.asarray(L0_attn_wo, f)),
        "gwT": tile_di(np.ascontiguousarray(np.asarray(L0_ffn_gate_w, f).T)),
        "uwT": tile_di(np.ascontiguousarray(np.asarray(L0_ffn_up_w, f).T)),
        "dwT": np.ascontiguousarray(np.asarray(L0_ffn_down_w, f).T),
        "gaT": tile_rows(np.ascontiguousarray(np.asarray(L0_ffn_gate_a, f).T)),
        "gbT": np.ascontiguousarray(np.asarray(L0_ffn_gate_b, f).T),
        "uaT": tile_rows(np.ascontiguousarray(np.asarray(L0_ffn_up_a, f).T)),
        "ubT": np.ascontiguousarray(np.asarray(L0_ffn_up_b, f).T),
        "daT": tile_rows(np.ascontiguousarray(np.asarray(L0_ffn_down_a, f).T)),
        "dbT": np.ascontiguousarray(np.asarray(L0_ffn_down_b, f).T),
    }
    in_maps = []
    for c in range(8):
        b, q4 = c // 4, c % 4
        thr = np.empty((128, 64), dtype=f)
        for qc in range(2):
            for kt in range(32):
                thr[:, qc * 32 + kt] = kt * 128 - 2048 * qc - q4
        m = dict(com)
        m["xT"] = xT[b]
        xqc = xT[b][:, q4::4]
        m["xq"] = np.ascontiguousarray(
            xqc.reshape(32, 128, 2, 512).transpose(1, 2, 0, 3).reshape(128, -1))
        m["thr"] = thr
        in_maps.append(m)

    nc = build_program()
    res = run_bass_kernel_spmd(nc, in_maps, core_ids=list(range(8)))
    outp = np.empty((B, T, DM), dtype=f)
    for c in range(8):
        b, q4 = c // 4, c % 4
        outp[b, q4::4, :] = res.results[c]["out"].T
    return outp
